# version 27
# baseline (speedup 1.0000x reference)
import numpy as np
from contextlib import ExitStack

import concourse.bass as bass
import concourse.mybir as mybir
from concourse.bass_utils import run_bass_kernel_spmd

F32 = mybir.dt.float32
BF16 = mybir.dt.bfloat16
AF = mybir.ActivationFunctionType
ALU = mybir.AluOpType
AX = mybir.AxisListType

NCORES = 8
TOK_CORE = 1024
T = 512
NPASS = TOK_CORE // T
D = 4096
NC = D // 128
H = 32
KV = 8
HD = 128
ISQ = float(1.0 / np.sqrt(128.0))
EPS = 1e-5
K_EXP = 64
RANK = 16
KR = K_EXP * RANK
NKRC = KR // 128
TOPK = 16
NT = T // 128
NBANK = T // 64
NGRP = T // 16


def build_nc():
    nc = bass.Bass("TRN2", target_bir_lowering=False)

    x0T = nc.dram_tensor("x0T", [D, TOK_CORE], F32, kind="ExternalInput").ap()
    wq = nc.dram_tensor("wq", [NC, 128, NC * 128], F32, kind="ExternalInput").ap()
    wk = nc.dram_tensor("wk", [KV, 128, NC * 128], F32, kind="ExternalInput").ap()
    wv = nc.dram_tensor("wv", [KV, 128, NC * 128], F32, kind="ExternalInput").ap()
    wo = nc.dram_tensor("wo", [NC, 128, NC * 128], F32, kind="ExternalInput").ap()
    aT = nc.dram_tensor("aT", [NKRC, 128, NC * 128], F32, kind="ExternalInput").ap()
    bkr = nc.dram_tensor("bkr", [NC, 128, NKRC * 128], F32, kind="ExternalInput").ap()
    wrT = nc.dram_tensor("wrT", [D, K_EXP], F32, kind="ExternalInput").ap()
    idf = nc.dram_tensor("idf", [128, 128], F32, kind="ExternalInput").ap()
    rsel = nc.dram_tensor("rsel", [K_EXP, KR], F32, kind="ExternalInput").ap()
    outT = nc.dram_tensor("outT", [D, TOK_CORE], F32, kind="ExternalOutput").ap()
    wc = nc.dram_tensor("wcache", [120, 128, 4096], BF16, kind="Internal").ap()

    with ExitStack() as stk:
        al = stk.enter_context
        r_fm = al(nc.sbuf_tensor("r_fm", [128, NC * T], BF16))
        x1 = al(nc.sbuf_tensor("x1", [128, NC * T], BF16))
        big = al(nc.sbuf_tensor("big", [128, NC * T], BF16))
        of = al(nc.sbuf_tensor("of", [128, NC * T], BF16))
        kf = al(nc.sbuf_tensor("kf", [128, KV * T], BF16))
        vf = al(nc.sbuf_tensor("vf", [128, KV * T], BF16))
        hf = al(nc.sbuf_tensor("hf", [128, NKRC * T], BF16))
        hgf = al(nc.sbuf_tensor("hgf", [128, NKRC * T], BF16))
        abd = al(nc.sbuf_tensor("abd", [128, 32 * (T // 4)], BF16))
        wbuf = al(nc.sbuf_tensor("wbuf", [128, 2 * 4096], BF16))
        wrs = al(nc.sbuf_tensor("wrs", [128, NC * K_EXP], BF16))
        rsels = al(nc.sbuf_tensor("rsels", [K_EXP, KR], BF16))
        idb = al(nc.sbuf_tensor("idb", [128, 128], BF16))
        idfs = al(nc.sbuf_tensor("idfs", [64, 64], F32))
        ones = al(nc.sbuf_tensor("ones", [128, 128], BF16))
        cst = al(nc.sbuf_tensor("cst", [128, 4], F32))
        mbf = al(nc.sbuf_tensor("mbf", [1, 2 * T], BF16))
        mrsave = al(nc.sbuf_tensor("mrsave", [1, T], F32))
        mr_bc = al(nc.sbuf_tensor("mr_bc", [128, T], BF16))
        rs_bc = al(nc.sbuf_tensor("rs_bc", [128, T], BF16))
        mr_bc4 = al(nc.sbuf_tensor("mr_bc4", [128, T], BF16))
        rs_bc4 = al(nc.sbuf_tensor("rs_bc4", [128, T], BF16))
        zsb = al(nc.sbuf_tensor("zsb", [128, 32], F32))
        zrr = al(nc.sbuf_tensor("zrr", [128, 32], F32))
        mxs = al(nc.sbuf_tensor("mxs", [128, 32], F32))
        stm = al(nc.sbuf_tensor("stm", [128, NT * 64], F32))
        swk = al(nc.sbuf_tensor("swk", [128, NT * 64], F32))
        msk = al(nc.sbuf_tensor("msk", [128, NT * 64], F32))
        mski = al(nc.sbuf_tensor("mski", [128, NT * 64], mybir.dt.int32))
        sel = al(nc.sbuf_tensor("sel", [128, NT * 64], F32))
        mxt = al(nc.sbuf_tensor("mxt", [128, NT], F32))
        nmx1 = al(nc.sbuf_tensor("nmx1", [128, NT], F32))
        gzs = al(nc.sbuf_tensor("gzs", [128, NT], F32))
        gzr = al(nc.sbuf_tensor("gzr", [128, NT], F32))
        gtm = al(nc.sbuf_tensor("gtm", [128, NT * 64], BF16))
        gfm = al(nc.sbuf_tensor("gfm", [K_EXP, T], BF16))
        ost = al(nc.sbuf_tensor("ost", [128, T], F32))
        abdt = hf[:, 0:2048]
        vstb = hf[:, 2048:2560]
        rowv = kf[:].bitcast(F32)

        pSC0 = al(nc.psum_tensor("pSC0", [128, 1024], F32))
        pSC1 = al(nc.psum_tensor("pSC1", [128, 1024], F32))
        pbB2 = al(nc.psum_tensor("pbB2", [128, 1024], F32))
        pbB = pbB2[:, 0:512]
        pbA = pSC0[:, 0:512]
        pbC = pSC0[:, 512:1024]
        pbS = pSC1[:, 0:512]
        pbQ = pSC1[:, 512:1024]
        ptT = al(nc.psum_tensor("ptT", [128, 1024], BF16))
        ptV2 = al(nc.psum_tensor("ptV2", [128, 384], BF16))
        ptV = ptV2[:, 0:256]
        ptG = ptV2[:, 256:384]

        spe = al(nc.semaphore("spe"))
        sdv = al(nc.semaphore("sdv"))
        sac = al(nc.semaphore("sac"))
        sgp = al(nc.semaphore("sgp"))
        NGCH = 11
        NSCH = 8
        gsem = [al(nc.semaphore("gch%d" % i)) for i in range(NGCH)]
        ssem = [al(nc.semaphore("sch%d" % i)) for i in range(NSCH)]
        block = al(nc.Block())

        cnt = {"spe": 0, "sdv": 0, "sac": 0, "sgp": 0}
        gcnt = [0] * NGCH
        scnt = [0] * NSCH
        sems = {"spe": spe, "sdv": sdv, "sac": sac, "sgp": sgp}

        EMIT = {"t": block.tensor, "v": block.vector, "s": block.scalar,
                "g": block.gpsimd, "y": block.sync}

        def _wait(e, deps):
            for d in deps:
                if d[0] == "g":
                    _, ch, n = d
                    if n:
                        e.wait_ge(gsem[ch], 16 * n)
                elif d[0] == "s2":
                    _, ch, n = d
                    if n:
                        e.wait_ge(ssem[ch], 16 * n)
                else:
                    sk, val = d
                    if val:
                        e.wait_ge(sems[sk], val)

        def ph(eng, deps, body):
            def f(e):
                _wait(e, deps)
                inst = body(e)
                if eng == "t":
                    inst.then_inc(spe, 1)
                elif eng == "v":
                    inst.then_inc(sdv, 1)
                elif eng == "s":
                    inst.then_inc(sac, 1)
                elif eng == "g":
                    inst.then_inc(sgp, 1)
            EMIT[eng](f)
            key = {"t": "spe", "v": "sdv", "s": "sac", "g": "sgp"}[eng]
            cnt[key] += 1
            return cnt[key]

        def dma(ch, deps, out, in_, accum=False):
            prev = gcnt[ch]
            def f(g):
                if prev:
                    g.wait_ge(gsem[ch], 16 * prev)
                _wait(g, deps)
                if accum:
                    g.dma_start(out=out, in_=in_,
                                accum_op=ALU.add).then_inc(gsem[ch], 16)
                else:
                    g.dma_start(out=out, in_=in_).then_inc(gsem[ch], 16)
            EMIT["g"](f)
            gcnt[ch] += 1
            return ("g", ch, gcnt[ch])

        def store(ch, deps, out, in_):
            prev = scnt[ch]
            def f(s):
                if prev:
                    s.wait_ge(ssem[ch], 16 * prev)
                _wait(s, deps)
                s.dma_start(out=out, in_=in_).then_inc(ssem[ch], 16)
            EMIT["y"](f)
            scnt[ch] += 1
            return ("s2", ch, scnt[ch])

        dma(0, [], idb[:], idf)
        dma(1, [], idfs[:], idf[0:64, 0:64])
        dma(2, [], rsels[:], rsel)
        m_const = dma(3, [], wrs[:].rearrange("p (c k) -> p c k", c=NC),
                      wrT.rearrange("(c p) k -> p c k", p=128))
        m_ones = ph("v", [], lambda v: v.memset(ones[:], 1.0))
        block.scalar(lambda s: s.add_instruction(
            mybir.InstLoadActFuncSet(
                name=nc.get_next_instruction_name(),
                act_func_set_id=6, ins=[], outs=[])))
        ph("v", [], lambda v: v.memset(cst[:, 0:1], -0.5))
        nc.const_aps.register(F32, -0.5, cst[:, 0:1])
        ph("v", [], lambda v: v.memset(cst[:, 1:2], ISQ))
        nc.const_aps.register(F32, ISQ, cst[:, 1:2])
        m_abd0 = ph("v", [], lambda v: v.memset(abd[:], 0.0))
        m_ptT0 = ph("v", [], lambda v: v.memset(ptT[:].bitcast(F32), 0.0))

        st8 = {"m_last_store": 0}

        def ln_stats(src, deps_src, eps, m_pe_stats=None,
                     chunk_ready=None, skip_mr_bc=False, save_mr=None,
                     add_mr=False, bc_out=None):
            if m_pe_stats is None:
                m_sq = {}
                for g in range(NC // 8):
                    sqb = [kf, vf][g % 2]
                    dd = list(deps_src)
                    if chunk_ready is not None:
                        cr = chunk_ready[8 * g + 7]
                        dd.append(cr if isinstance(cr, tuple) else ("sdv", cr))
                    if g >= 2:
                        dd.append(("spe", m_sq[g - 2]))
                    def sqg(v, g=g, sqb=sqb):
                        for i in range(8):
                            c = 8 * g + i
                            inst = v.tensor_tensor(
                                sqb[:, T * i:T * i + T],
                                src[:, T * c:T * c + T],
                                src[:, T * c:T * c + T], ALU.mult)
                        return inst
                    m_dv = ph("v", dd, sqg)
                    def smg(t, g=g, sqb=sqb):
                        for i in range(8):
                            c = 8 * g + i
                            t.matmul(pbS[0:1, :], ones[:, 0:1],
                                     src[:, T * c:T * c + T],
                                     start=(c == 0), stop=(c == NC - 1))
                        for i in range(8):
                            c = 8 * g + i
                            inst = t.matmul(pbQ[0:1, :], ones[:, 0:1],
                                            sqb[:, T * i:T * i + T],
                                            start=(c == 0), stop=(c == NC - 1))
                        return inst
                    dd2 = [("sdv", m_dv)]
                    if g == 0:
                        dd2.append(("sdv", m_ones))
                    m_sq[g] = ph("t", dd2, smg)
                m_pe = cnt["spe"]
            else:
                m_pe = m_pe_stats
            R = T
            rm = rowv[0:1, 0:T]
            rt = rowv[0:1, T:2 * T]
            ra = rowv[0:1, 2 * T:3 * T]
            m1 = ph("v", [("spe", m_pe)], lambda v: v.tensor_scalar(
                rm, pbS[0:1, :], 1.0 / D, None, ALU.mult))
            def e2m2(v):
                v.tensor_scalar(rt, pbQ[0:1, :], 1.0 / D, float(eps),
                                ALU.mult, ALU.add)
                return v.tensor_tensor(ra, rm, rm, ALU.mult)
            m3 = ph("v", [("sdv", m1)], e2m2)
            m4 = ph("v", [("sdv", m3)], lambda v: v.tensor_tensor(
                rt, rt, ra, ALU.subtract))
            m5 = ph("s", [("sdv", m4)], lambda s: s.activation(rt, rt, AF.Ln))
            m6 = ph("s", [("sac", m5)], lambda s: s.activation(
                rt, rt, AF.Exp, scale=-0.5))
            m7 = ph("v", [("sac", m6)], lambda v: v.tensor_tensor(
                ra, rm, rt, ALU.mult))
            if save_mr is not None:
                m7 = ph("v", [("sdv", m7)], lambda v: v.tensor_copy(
                    mrsave[0:1, :], ra))
            if add_mr:
                m7 = ph("v", [("sdv", m7)], lambda v: v.tensor_tensor(
                    ra, ra, mrsave[0:1, :], ALU.add))
            m8b = ph("v", [("sdv", m7)],
                     lambda v: v.tensor_copy(mbf[:, R:2 * R], rt))
            mb2 = ph("t", [("sdv", m8b)], lambda t: t.matmul(
                pbB[:, :], ones[0:1, :], mbf[0:1, R:2 * R], start=True,
                stop=True))
            mr_dst, rs_dst = bc_out if bc_out is not None else (mr_bc, rs_bc)
            me2m = ph("v", [("spe", mb2)],
                      lambda v: v.tensor_copy(rs_dst[:], pbB[:]))
            if skip_mr_bc:
                return me2m
            m8a = ph("v", [("sdv", m7)],
                     lambda v: v.tensor_copy(mbf[:, 0:R], ra))
            mb1 = ph("t", [("sdv", m8a)], lambda t: t.matmul(
                pbC[:, :], ones[0:1, :], mbf[0:1, 0:R], start=True, stop=True))
            me1 = ph("v", [("spe", mb1), ("sdv", me2m)],
                     lambda v: v.tensor_copy(mr_dst[:], pbC[:]))
            return me1

        def ln_normalize(src, dst, m_bc, chunk_guard=None):
            def p1(v):
                for c in range(NC):
                    if chunk_guard is not None:
                        tok = chunk_guard[c // (NC // 4)]
                        v.wait_ge(gsem[tok[1]], 16 * tok[2])
                    inst = v.tensor_tensor(dst[:, T * c:T * c + T],
                                           src[:, T * c:T * c + T], rs_bc[:],
                                           ALU.mult)
                return inst
            m1 = ph("v", [("sdv", m_bc)], p1)
            base = cnt["sdv"]
            def p2(v):
                v.wait_ge(sdv, m1)
                for c in range(NC):
                    v.tensor_tensor(dst[:, T * c:T * c + T],
                                    dst[:, T * c:T * c + T], mr_bc[:],
                                    ALU.subtract).then_inc(sdv, 1)
            EMIT["v"](p2)
            cnt["sdv"] += NC
            return cnt["sdv"], base

        def ln_residual(rbuf, y, m_bc):
            abs_c = {"n": cnt["sdv"]}
            spe_base = cnt["spe"]
            p2_mark = {}
            p3_mark = {}
            sqg_mark = {}
            def pall(v):
                v.wait_ge(sdv, m_bc)
                def p3(c):
                    v.wait_ge(sdv, p2_mark[c])
                    v.tensor_tensor(rbuf[:, T * c:T * c + T],
                                    rbuf[:, T * c:T * c + T],
                                    y[:, T * c:T * c + T],
                                    ALU.add).then_inc(sdv, 1)
                    abs_c["n"] += 1
                    p3_mark[c] = abs_c["n"]
                    if c % 8 == 7:
                        g = c // 8
                        sqb = [kf, vf][g % 2]
                        if g >= 2:
                            v.wait_ge(spe, spe_base + g - 1)
                        for i in range(8):
                            cc = 8 * g + i
                            v.wait_ge(sdv, p3_mark[cc])
                            v.tensor_tensor(sqb[:, T * i:T * i + T],
                                            rbuf[:, T * cc:T * cc + T],
                                            rbuf[:, T * cc:T * cc + T],
                                            ALU.mult).then_inc(sdv, 1)
                            abs_c["n"] += 1
                        sqg_mark[g] = abs_c["n"]
                last = 0
                for c in range(NC):
                    v.tensor_tensor(y[:, T * c:T * c + T],
                                    y[:, T * c:T * c + T], rs_bc[:],
                                    ALU.mult).then_inc(sdv, 1)
                    abs_c["n"] += 1
                    p2_mark[c] = abs_c["n"]
                    if c >= 1:
                        p3(c - 1)
                        last = c
                for c in range(last, NC):
                    p3(c)
            EMIT["v"](pall)
            cnt["sdv"] = abs_c["n"]
            return cnt["sdv"], sqg_mark, spe_base

        def ln3_stats_mms(sqg_mark, rbuf, spe_base):
            assert cnt["spe"] == spe_base, (cnt["spe"], spe_base)
            m_g = {}
            for g in range(NC // 8):
                sqb = [kf, vf][g % 2]
                def smg(t, g=g, sqb=sqb):
                    for i in range(8):
                        c = 8 * g + i
                        t.matmul(pbS[0:1, :], ones[:, 0:1],
                                 rbuf[:, T * c:T * c + T],
                                 start=(c == 0), stop=(c == NC - 1))
                    for i in range(8):
                        c = 8 * g + i
                        inst = t.matmul(pbQ[0:1, :], ones[:, 0:1],
                                        sqb[:, T * i:T * i + T],
                                        start=(c == 0), stop=(c == NC - 1))
                    return inst
                m_g[g] = ph("t", [("sdv", sqg_mark[g])], smg)
            return m_g[NC // 8 - 1]

        wb_guard = {"m": 0}
        wst_guard = {}

        def mm_stage(w_dram, njb, kchunks, rhs_fn, evac_fn, deps_rhs,
                     m_prev_bank=None, stats_dst=None, rhs_chunk=None,
                     cache=None, slots=None, slot_deps=None, evac_eng="v",
                     banks=None, jb_deps=None, merge2=False):
            m_dma_jb, m_mm_jb, m_ev_jb = {}, {}, {}
            m_sqmm = {}
            m_ws = {}
            slot, cmode = cache if cache is not None else (None, None)
            if banks is None:
                banks = [pbA, pbB, pbC]
            if slots is None:
                wslots = [wbuf[:, 0:kchunks * 128],
                          wbuf[:, 4096:4096 + kchunks * 128]]
            else:
                wslots = slots
            ns = len(wslots)
            for jb in range(njb):
                sidx = jb % ns
                wb = wslots[sidx]
                dd = []
                if jb >= ns:
                    dd.append(("spe", m_mm_jb[jb - ns]))
                    if cmode == "write":
                        dd.append(m_ws[jb - ns])
                else:
                    dd.append(("spe", wb_guard["m"]))
                    dd += list(wst_guard.values())
                    if slot_deps and sidx in slot_deps:
                        dd += list(slot_deps[sidx])
                if cmode == "read":
                    m_dma_jb[jb] = store(sidx, dd, wb,
                                         wc[slot + jb, :, 0:kchunks * 128])
                else:
                    m_dma_jb[jb] = dma(sidx, dd, wb, w_dram[jb, :, :])
                    if cmode == "write":
                        m_ws[jb] = store(sidx, [m_dma_jb[jb]],
                                         wc[slot + jb, :, 0:kchunks * 128], wb)
                        wst_guard[sidx] = m_ws[jb]
                bank = banks[jb % len(banks)]
                def mms(t, wb=wb, bank=bank, kchunks=kchunks, jb=jb):
                    for c in range(kchunks):
                        if rhs_chunk is not None and jb < 2:
                            t.wait_ge(sdv, rhs_chunk + c + 1)
                        inst = t.matmul(bank[:, :],
                                        wb[:, 128 * c:128 * c + 128],
                                        rhs_fn(c), start=(c == 0),
                                        stop=(c == kchunks - 1))
                    return inst
                dd = [m_dma_jb[jb]] + list(deps_rhs)
                if jb_deps and jb in jb_deps:
                    dd += list(jb_deps[jb])
                if jb >= len(banks):
                    dd.append(({"v": "sdv", "s": "sac"}[evac_eng],
                               m_ev_jb[jb - len(banks)]))
                elif m_prev_bank is not None:
                    dd.append(m_prev_bank)
                if merge2 and jb % 2 == 0:
                    pend = (dd, mms)
                    m_mm_jb[jb] = cnt["spe"] + 1
                elif merge2:
                    pdd, pmms = pend
                    def both(t, pmms=pmms, mms=mms):
                        pmms(t)
                        return mms(t)
                    m_mm_jb[jb] = ph("t", pdd + dd, both)
                else:
                    m_mm_jb[jb] = ph("t", dd, mms)
                m_ev_jb[jb] = ph(evac_eng, [("spe", m_mm_jb[jb])],
                                 lambda v, jb=jb, bank=bank: evac_fn(v, jb, bank))
                ek = {"v": "sdv", "s": "sac"}[evac_eng]
                if stats_dst is not None and jb % 8 == 7:
                    g = jb // 8
                    sqb = [kf, vf][g % 2]
                    dd = [(ek, m_ev_jb[jb])]
                    if g >= 2:
                        dd.append(("spe", m_sqmm[g - 2]))
                    def sqg(v, g=g, sqb=sqb):
                        for i in range(8):
                            c = 8 * g + i
                            inst = v.tensor_tensor(
                                sqb[:, T * i:T * i + T],
                                stats_dst[:, T * c:T * c + T],
                                stats_dst[:, T * c:T * c + T], ALU.mult)
                        return inst
                    m_sqdv = ph("v", dd, sqg)
                    def smg(t, g=g, sqb=sqb):
                        for i in range(8):
                            c = 8 * g + i
                            t.matmul(pbS[0:1, :], ones[:, 0:1],
                                     stats_dst[:, T * c:T * c + T],
                                     start=(c == 0), stop=(c == njb - 1))
                        for i in range(8):
                            c = 8 * g + i
                            inst = t.matmul(pbQ[0:1, :], ones[:, 0:1],
                                            sqb[:, T * i:T * i + T],
                                            start=(c == 0), stop=(c == njb - 1))
                        return inst
                    m_sqmm[g] = ph("t", [("sdv", m_sqdv)], smg)
            wb_guard["m"] = m_mm_jb[njb - 1]
            if stats_dst is not None:
                return m_ev_jb[njb - 1], m_sqmm[njb // 8 - 1]
            return m_ev_jb[njb - 1]

        m_prev_h = 0
        for ps in range(NPASS):
            t0 = T * ps
            R = [r_fm, x1][ps % 2]
            X = [x1, r_fm][ps % 2]

            if ps == 0:
                x0_toks = []
                for q in range(4):
                    c0 = 8 * q
                    x0_toks.append(dma(
                        8, [],
                        R[:].rearrange("p (c t) -> p c t", c=NC)[
                            :, c0:c0 + 8, :],
                        x0T.rearrange("(c p) t -> p c t", p=128)[
                            :, c0:c0 + 8, t0:t0 + T]))
                m_x0 = x0_toks[-1]
            else:
                m_x0 = st8["m_x0_next"]
                x0_toks = [m_x0] * 4
            deps_r = [m_x0]

            gdeps = [("spe", st8["m_st4_prev"])] if ps > 0 else []
            m_bc1 = ln_stats(R, gdeps, EPS,
                             chunk_ready=[x0_toks[c // 8] for c in range(NC)])
            if ps > 0:
                st8["emit_s12"]()
            m_x1, b_x1 = ln_normalize(
                R, X, m_bc1,
                chunk_guard=st8["b_addp_prev"] if ps else None)

            def evac_k(s, jb, bank):
                dst = kf[:].rearrange("p (t g) -> p g t", g=KV)[:, jb, :]
                return s.copy(dst, bank[:, :])
            m_k = mm_stage(wk, KV, NC, lambda c: X[:, T * c:T * c + T],
                           evac_k, [], rhs_chunk=b_x1, evac_eng="s",
                           slots=[wbuf[:, 0:4096], wbuf[:, 4096:8192], hf[:]],
                           slot_deps={2: [("sdv", st8.get("m_hg_prev", 0))]})
            def evac_v(s, jb, bank):
                dst = vf[:].rearrange("p (t g) -> p g t", g=KV)[:, jb, :]
                return s.copy(dst, bank[:, :])
            m_v = mm_stage(wv, KV, NC, lambda c: X[:, T * c:T * c + T],
                           evac_v, [("sdv", m_x1)], m_prev_bank=("sac", m_k),
                           evac_eng="s",
                           slots=[wbuf[:, 0:4096], wbuf[:, 4096:8192], hf[:]])
            def evac_q(s, jb, bank):
                if ps > 0:
                    tok = st8["m_out_add"]
                    s.wait_ge(gsem[tok[1]], 16 * tok[2])
                dst = big[:].rearrange("p (t h) -> p h t", h=H)[:, jb, :]
                return s.mul(dst, bank[:, :], ISQ)
            m_q = mm_stage(wq, H, NC, lambda c: X[:, T * c:T * c + T],
                           evac_q, [("sdv", m_x1)], m_prev_bank=("sac", m_v),
                           evac_eng="s",
                           slots=[wbuf[:, 0:4096], wbuf[:, 4096:8192], hf[:]],
                           slot_deps={2: [("sdv", st8.get("m_hg_prev", 0))]})

            NB2 = 4
            m_sm_done = {}
            m_abd_bank = {}
            m_ev_grp, m_av_grp, m_oev_grp = {}, {}, {}
            m_vev_grp = {}

            def emit_softmax(b2, bank, m_sc):
                def smax(v, bank=bank):
                    for u4 in range(4):
                        sv = bank[32 * u4:32 * u4 + 32, :].rearrange(
                            "p (w g) -> p w g", g=32)[:, :, 8 * u4:8 * u4 + 8]
                        inst = v.reduce_max(mxs[32 * u4:32 * u4 + 32, :], sv,
                                            axis=AX.X)
                    return inst
                m_mx = ph("v", [("spe", m_sc)], smax)
                def ssub(v, bank=bank):
                    for u4 in range(4):
                        sv = bank[32 * u4:32 * u4 + 32, :].rearrange(
                            "p (w g) -> p w g", g=32)[:, :, 8 * u4:8 * u4 + 8]
                        mb = mxs[32 * u4:32 * u4 + 32, :].unsqueeze(
                            2).broadcast_to([32, 32, 8])
                        inst = v.tensor_tensor(sv, sv, mb, ALU.subtract)
                    return inst
                m_sub = ph("v", [("sdv", m_mx)], ssub)
                def sexp(s, b2=b2, bank=bank):
                    for u4 in range(4):
                        sv = bank[32 * u4:32 * u4 + 32, :].rearrange(
                            "p (w g) -> p w g", g=32)[:, :, 8 * u4:8 * u4 + 8]
                        av = abd[32 * u4:32 * u4 + 32,
                                 1024 * b2:1024 * b2 + 1024].rearrange(
                            "p (w g) -> p w g", g=32)[:, :, 8 * u4:8 * u4 + 8]
                        inst = s.activation(av, sv, AF.Exp)
                    return inst
                dd = [("sdv", m_sub)]
                if ps == 0 and b2 == 0:
                    dd.append(("sdv", m_abd0))
                m_sm_done[b2] = ph("s", dd, sexp)
                def ssum(v, b2=b2):
                    for u4 in range(4):
                        av = abd[32 * u4:32 * u4 + 32,
                                 1024 * b2:1024 * b2 + 1024].rearrange(
                            "p (w g) -> p w g", g=32)[:, :, 8 * u4:8 * u4 + 8]
                        inst = v.reduce_sum(zsb[32 * u4:32 * u4 + 32, :], av,
                                            axis=AX.X)
                    return inst
                m_sum = ph("v", [("sac", m_sm_done[b2])], ssum)
                m_rec = ph("v", [("sdv", m_sum)],
                           lambda v: v.reciprocal(zrr[:, :], zsb[:, :]))
                def smul(v, b2=b2):
                    for u4 in range(4):
                        av = abd[32 * u4:32 * u4 + 32,
                                 1024 * b2:1024 * b2 + 1024].rearrange(
                            "p (w g) -> p w g", g=32)[:, :, 8 * u4:8 * u4 + 8]
                        zb = zrr[32 * u4:32 * u4 + 32, :].unsqueeze(
                            2).broadcast_to([32, 32, 8])
                        inst = v.tensor_tensor(av, av, zb, ALU.mult)
                    return inst
                m_abd_bank[b2] = ph("v", [("sdv", m_rec)], smul)

            def emit_grps(b2):
                m_abd_b = m_abd_bank[b2]
                for gp in range(4):
                    pair = 4 * b2 + gp
                    half = pair % 2
                    def trv(t, b2=b2, gp=gp, pair=pair):
                        if pair >= 1:
                            t.wait_ge(sdv, m_vev_grp[pair - 1])
                        for g2 in range(2):
                            grp = 8 * b2 + 2 * gp + g2
                            inst = t.transpose(ptV[:, 128 * g2:128 * g2 + 128],
                                               vf[:, 128 * grp:128 * grp + 128],
                                               idb[:])
                        return inst
                    m_trv = ph("t", [("sac", m_v)], trv)
                    def evv(v, pair=pair, half=half, m_trv=m_trv):
                        if pair >= 1:
                            v.wait_ge(spe, m_av_grp[pair - 1])
                        v.wait_ge(spe, m_trv)
                        return v.tensor_copy(
                            vstb[:, 256 * half:256 * half + 256], ptV[:])
                    m_vev_grp[pair] = ph("v", [], evv)
                    def trs(t, b2=b2, gp=gp, pair=pair, m_abd_b=m_abd_b):
                        if pair >= 1:
                            t.wait_ge(sdv, m_ev_grp[pair - 1])
                        elif ps == 0 and b2 == 0:
                            t.wait_ge(sdv, m_ptT0)
                        for g2 in range(2):
                            grp = 8 * b2 + 2 * gp + g2
                            for j in range(4):
                                w = 4 * grp + j
                                inst = t.transpose(
                                    ptT[32 * j:32 * j + 32,
                                        512 * g2 + 128 * j:512 * g2 + 128 * j + 128],
                                    abd[:, 32 * w:32 * w + 32], idb[:],
                                    tile_position=(0, 32 * j))
                        return inst
                    m_tr = ph("t", [("sdv", m_abd_b)], trs)
                    def evg(v, pair=pair, half=half, m_tr=m_tr):
                        if pair >= 1:
                            v.wait_ge(spe, m_av_grp[pair - 1])
                        v.wait_ge(spe, m_tr)
                        return v.tensor_copy(
                            abdt[:, 1024 * half:1024 * half + 1024], ptT[:])
                    m_ev_grp[pair] = ph("v", [], evg)
                    def avm(t, pair=pair, half=half):
                        if pair >= 1:
                            t.wait_ge(sac, m_oev_grp[pair - 1])
                        for g2 in range(2):
                            inst = t.matmul(
                                pbB2[:, 512 * g2:512 * g2 + 512],
                                vstb[:, 256 * half + 128 * g2:
                                     256 * half + 128 * g2 + 128],
                                abdt[:, 1024 * half + 512 * g2:
                                     1024 * half + 512 * g2 + 512],
                                start=True, stop=True)
                        return inst
                    m_av_grp[pair] = ph("t", [("sdv", m_ev_grp[pair]),
                                               ("sdv", m_vev_grp[pair])], avm)
                    def oev(s, pair=pair):
                        src_v = pbB2[:].rearrange(
                            "p (g t h) -> p h g t", g=2, t=16)
                        dst_v = of[:].rearrange("p (h q) -> p h q", h=H)[
                            :, :, 32 * pair:32 * pair + 32].rearrange(
                            "p h (g t) -> p h g t", g=2)
                        return s.copy(dst_v, src_v)
                    m_oev_grp[pair] = ph("s", [("spe", m_av_grp[pair])], oev)

            SB2 = [pSC0, pSC1]
            for b2 in range(NB2):
                bank = SB2[b2 % 2]
                def scmm(t, b2=b2, bank=bank):
                    for u in range(128):
                        tok = 128 * b2 + u
                        u4, w = u % 4, u // 4
                        inst = t.matmul(
                            bank[32 * u4:32 * u4 + 32,
                                 32 * w + 8 * u4:32 * w + 8 * u4 + 8],
                            big[:, H * tok:H * tok + H],
                            kf[:, KV * tok:KV * tok + KV],
                            start=True, stop=True, tile_position=(0, 32 * u4))
                    return inst
                dd = [("sac", m_q), ("sac", m_k)]
                if b2 >= 2:
                    dd.append(("sac", m_sm_done[b2 - 2]))
                m_sc = ph("t", dd, scmm)
                if b2 >= 1:
                    emit_grps(b2 - 1)
                emit_softmax(b2, bank, m_sc)
            emit_grps(NB2 - 1)
            m_attn_s = m_oev_grp[NGRP // 2 - 1]

            def evac_y(s, jb, bank):
                return s.copy(big[:, T * jb:T * jb + T], bank[:, :])
            m_y, m_st2 = mm_stage(wo, NC, NC, lambda c: of[:, T * c:T * c + T],
                                  evac_y, [("sac", m_attn_s)], stats_dst=big,
                                  evac_eng="s",
                                  slots=[wbuf[:, 0:4096], wbuf[:, 4096:8192],
                                         hf[:]],
                                  slot_deps={2: [("spe",
                                                  m_av_grp[NGRP // 2 - 1])]})
            m_bc2 = ln_stats(big, [("sac", m_y)], EPS, m_pe_stats=m_st2,
                             skip_mr_bc=True, save_mr=True)
            m_res2, sqg3, spe_b3 = ln_residual(R, big, m_bc2)

            m_st3 = ln3_stats_mms(sqg3, R, spe_b3)
            m_bc3 = ln_stats(R, [], EPS, m_pe_stats=m_st3)
            m_x3, b_x3 = ln_normalize(R, X, m_bc3)

            def rmm(t):
                for c in range(NC):
                    t.wait_ge(sdv, b_x3 + c + 1)
                    inst = t.matmul(pbA[0:K_EXP, :],
                                    wrs[:, K_EXP * c:K_EXP * c + K_EXP],
                                    X[:, T * c:T * c + T],
                                    start=(c == 0), stop=(c == NC - 1))
                return inst
            m_rt = ph("t", [m_const], rmm)
            m_rev = ph("v", [("spe", m_rt)],
                       lambda v: v.tensor_copy(ost[0:K_EXP, 0:T], pbA[0:K_EXP, :]))
            def rtp(t):
                for i in range(NT):
                    inst = t.transpose(pbC[:, 64 * i:64 * i + 64],
                                       ost[0:K_EXP, 128 * i:128 * i + 128],
                                       idfs[:, :])
                return inst
            m_rtp = ph("t", [("sdv", m_rev)], rtp)
            def rtm(v):
                v.tensor_copy(stm[:], pbC[:, 0:NT * 64])
                return v.tensor_copy(swk[:], pbC[:, 0:NT * 64])
            m_stm = ph("v", [("spe", m_rtp)], rtm)

            def evac_h(s, jb, bank):
                return s.copy(hf[:, T * jb:T * jb + T], bank[:, :])
            m_h = mm_stage(aT, NKRC, NC, lambda c: X[:, T * c:T * c + T],
                           evac_h, [], rhs_chunk=b_x3,
                           m_prev_bank=("sdv", m_rev),
                           cache=(80, "write" if ps == 0 else "read"),
                           evac_eng="s", banks=[pbA, pbB],
                           slots=[wbuf[:, 0:4096], wbuf[:, 4096:8192],
                                  of[:, 0:4096]])

            m_memneg = ph("v", [], lambda v: v.memset(sel[:], -1e30))
            sw3f = swk[:].rearrange("p (i k) -> p i k", k=64)
            st3f = stm[:].rearrange("p (i k) -> p i k", k=64)
            mi3f = mski[:].rearrange("p (i k) -> p i k", k=64)
            mxbf = mxt[:, :].unsqueeze(2).broadcast_to([128, NT, 64])
            m_nm1 = ph("v", [("sdv", m_stm)],
                       lambda v: v.reduce_max(gzs[:, :], st3f, axis=AX.X))
            m_nm2 = ph("v", [("sdv", m_nm1)], lambda v: v.tensor_scalar(
                nmx1[:], gzs[:], -1.0, None, ALU.mult))
            m_cur = m_stm
            round_marks = []
            for rnd in range(TOPK):
                m1 = ph("v", [("sdv", m_cur), ("sdv", m_memneg)],
                        lambda v: v.reduce_max(mxt[:, :], sw3f, axis=AX.X))
                m2 = ph("v", [("sdv", m1)],
                        lambda v: v.tensor_tensor(mi3f, sw3f, mxbf, ALU.is_ge))
                m_cur = ph("v", [("sdv", m2)],
                           lambda v: v.copy_predicated(swk[:], mski[:],
                                                       sel[:]))
                round_marks.append(m_cur)
            chain = {0: m_cur, 1: m_cur}
            m_topk = ph("v", [("sdv", chain[0]), ("sdv", chain[1])],
                        lambda v: v.tensor_scalar(msk[:], swk[:], -1e29, None,
                                                  ALU.is_lt))
            stm3 = stm[:].rearrange("p (i k) -> p i k", k=64)
            swk3 = swk[:].rearrange("p (i k) -> p i k", k=64)
            msk3 = msk[:].rearrange("p (i k) -> p i k", k=64)
            gtm3 = gtm[:].rearrange("p (i k) -> p i k", k=64)
            nmxb = nmx1[:, :].unsqueeze(2).broadcast_to([128, NT, 64])
            gzrb = gzr[:, :].unsqueeze(2).broadcast_to([128, NT, 64])
            m_gei = ph("v", [("sdv", m_topk)],
                       lambda v: v.tensor_tensor(swk3, stm3, nmxb, ALU.add))
            m_ge = ph("s", [("sdv", m_gei)],
                      lambda s: s.activation(swk[:], swk[:], AF.Exp))
            m_gm = ph("v", [("sac", m_ge)], lambda v: v.tensor_tensor(
                swk[:], swk[:], msk[:], ALU.mult))
            m_gs = ph("v", [("sdv", m_gm)],
                      lambda v: v.reduce_sum(gzs[:, :], swk3, axis=AX.X))
            m_gr = ph("v", [("sdv", m_gs)],
                      lambda v: v.reciprocal(gzr[:], gzs[:]))
            m_gt = ph("v", [("sdv", m_gr)],
                      lambda v: v.tensor_tensor(gtm3, swk3, gzrb, ALU.mult))
            m_gf = 0
            for i in range(NT):
                def gtp(t, i=i, m_gt=m_gt, m_gf=m_gf):
                    t.wait_ge(sdv, m_gt)
                    if i >= 1:
                        t.wait_ge(sdv, m_gf)
                    return t.transpose(ptG[0:K_EXP, :],
                                       gtm[:, 64 * i:64 * i + 64], idb[:])
                m_g1 = ph("t", [], gtp)
                m_gf = ph("v", [("spe", m_g1)], lambda v, i=i: v.tensor_copy(
                    gfm[:, 128 * i:128 * i + 128], ptG[0:K_EXP, :]))
            m_gfm = m_gf

            m_grc = {}
            for c in range(NKRC):
                bank = [pbA, pbC][c % 2]
                def grm(t, c=c, bank=bank):
                    if c >= 2:
                        t.wait_ge(sdv, m_grc[c - 2])
                    return t.matmul(bank[:, :],
                                    rsels[0:K_EXP, 128 * c:128 * c + 128],
                                    gfm[0:K_EXP, :], start=True, stop=True)
                m_g2 = ph("t", [("sdv", m_gfm), ("sac", m_h)], grm)
                m_grc[c] = ph("v", [("spe", m_g2)],
                              lambda v, c=c, bank=bank: v.tensor_tensor(
                                  hgf[:, T * c:T * c + T],
                                  hf[:, T * c:T * c + T], bank[:, :], ALU.mult))
            m_hg = m_grc[NKRC - 1]

            def evac_y2(s, jb, bank):
                return s.copy(big[:, T * jb:T * jb + T], bank[:, :])
            m_y2, m_st4 = mm_stage(bkr, NC, NKRC,
                                   lambda c: hgf[:, T * c:T * c + T],
                                   evac_y2, [("sdv", m_hg)], stats_dst=big,
                                   evac_eng="s", merge2=True,
                                   cache=(88, "write" if ps == 0 else "read"),
                                   slots=[wbuf[:, 1024 * i:1024 * i + 1024]
                                          for i in range(8)])
            if ps + 1 < NPASS:
                st8["m_x0_next"] = dma(
                    8, [("sac", m_h)],
                    X[:].rearrange("p (c t) -> p c t", c=NC),
                    x0T.rearrange("(c p) t -> p c t", p=128)[
                        :, :, t0 + T:t0 + 2 * T])

            m_bc4 = ln_stats(big, [("sac", m_y2)], EPS / 4.0,
                             m_pe_stats=m_st4, add_mr=True,
                             bc_out=(mr_bc4, rs_bc4))
            QW = NC // 4
            m_r_dma = {}
            outv = outT.rearrange("(c p) t -> p c t", p=128)
            for q in range(4):
                c0 = QW * q
                m_r_dma[q] = dma(9, [("sdv", m_res2)] if q == 0 else [],
                                 outv[:, c0:c0 + QW, t0:t0 + T],
                                 R[:].rearrange("p (c t) -> p c t", c=NC)[
                                     :, c0:c0 + QW, :])
            st8["b_addp_prev"] = m_r_dma

            def emit_s12_tail(Rc=R, bigc=big, m_bc4c=m_bc4, t0c=t0,
                              m_r_dmac=m_r_dma):
                def p1(v):
                    for c in range(NC):
                        inst = v.tensor_tensor(bigc[:, T * c:T * c + T],
                                               bigc[:, T * c:T * c + T],
                                               rs_bc4[:], ALU.mult)
                    return inst
                m_p1 = ph("v", [("sdv", m_bc4c)], p1)
                b_p2 = cnt["sdv"]
                def p2(v):
                    v.wait_ge(sdv, m_p1)
                    for c in range(NC):
                        v.tensor_tensor(bigc[:, T * c:T * c + T],
                                        bigc[:, T * c:T * c + T], mr_bc4[:],
                                        ALU.subtract).then_inc(sdv, 1)
                EMIT["v"](p2)
                cnt["sdv"] += NC
                m_add_last = 0
                for q in range(4):
                    c0 = QW * q
                    m_add_last = dma(
                        10, [("sdv", b_p2 + c0 + QW), m_r_dmac[q]],
                        outv[:, c0:c0 + QW, t0c:t0c + T],
                        bigc[:].rearrange("p (c t) -> p c t", c=NC)[
                            :, c0:c0 + QW, :],
                        accum=True)
                st8["m_out_add"] = m_add_last
            st8["emit_s12"] = emit_s12_tail
            st8["m_st4_prev"] = m_st4
            m_prev_h = m_h
            st8["m_hg_prev"] = m_hg
        st8["emit_s12"]()

        @block.sync
        def _(s):
            for i in range(NGCH):
                if gcnt[i]:
                    s.wait_ge(gsem[i], 16 * gcnt[i])
            for i in range(NSCH):
                if scnt[i]:
                    s.wait_ge(ssem[i], 16 * scnt[i])

    return nc


_NC_CACHE = None


def kernel(L0_attn_hidden_states, L0_attn_Wq, L0_attn_Wk, L0_attn_Wv,
           L0_attn_Wo, L0_ffn_A, L0_ffn_B, L0_ffn_Wr):
    global _NC_CACHE
    x = np.asarray(L0_attn_hidden_states, dtype=np.float32)
    wq = np.ascontiguousarray(np.asarray(L0_attn_Wq, dtype=np.float32))
    wk = np.ascontiguousarray(np.asarray(L0_attn_Wk, dtype=np.float32))
    wv = np.ascontiguousarray(np.asarray(L0_attn_Wv, dtype=np.float32))
    wo = np.ascontiguousarray(np.asarray(L0_attn_Wo, dtype=np.float32))
    A = np.asarray(L0_ffn_A, dtype=np.float32)
    B = np.asarray(L0_ffn_B, dtype=np.float32)
    Wr = np.asarray(L0_ffn_Wr, dtype=np.float32)

    def blk(W):
        Din, Dout = W.shape
        nc_, njb = Din // 128, Dout // 128
        return np.ascontiguousarray(
            W.reshape(nc_, 128, njb, 128).transpose(2, 1, 0, 3).reshape(
                njb, 128, nc_ * 128))

    x0T = np.ascontiguousarray(x.reshape(8192, 4096).T)
    wq = blk(wq)
    wk = blk(wk)
    wv = blk(wv)
    wo = blk(wo)
    aT = blk(np.ascontiguousarray(A.reshape(KR, D).T))
    bkr = blk(np.ascontiguousarray(B.transpose(0, 2, 1).reshape(KR, D)))
    wrT = np.ascontiguousarray(Wr.T)
    idf = np.eye(128, dtype=np.float32)
    rsel = np.zeros((K_EXP, KR), np.float32)
    for c in range(NKRC):
        for p in range(128):
            rsel[8 * c + p // 16, 128 * c + p] = 1.0

    if _NC_CACHE is None:
        _NC_CACHE = build_nc()
    nc = _NC_CACHE

    in_maps = []
    for core in range(NCORES):
        sl = np.ascontiguousarray(x0T[:, TOK_CORE * core:TOK_CORE * (core + 1)])
        in_maps.append({
            "x0T": sl, "wq": wq, "wk": wk, "wv": wv, "wo": wo,
            "aT": aT, "bkr": bkr, "wrT": wrT, "idf": idf, "rsel": rsel,
        })
    res = run_bass_kernel_spmd(nc, in_maps, core_ids=list(range(NCORES)))
    outT = np.concatenate([res.results[c]["outT"] for c in range(NCORES)],
                          axis=1)
    out = np.ascontiguousarray(outT.T).reshape(4, 2048, 4096).astype(np.float32)
    return out


if __name__ == "__main__":
    import reference as _ref
    inputs = _ref.setup_inputs()
    out = kernel(**{k: np.asarray(v) for k, v in inputs.items()})
    ref = np.asarray(_ref.reference(**inputs))
    num = np.sqrt(((out - ref) ** 2).mean())
    den = np.sqrt((ref ** 2).mean())
    print("rel l2:", num / den)


# version 28
# speedup vs baseline: 1.0198x; 1.0198x over previous
import numpy as np
from contextlib import ExitStack

import concourse.bass as bass
import concourse.mybir as mybir
from concourse.bass_utils import run_bass_kernel_spmd

F32 = mybir.dt.float32
BF16 = mybir.dt.bfloat16
AF = mybir.ActivationFunctionType
ALU = mybir.AluOpType
AX = mybir.AxisListType

NCORES = 8
TOK_CORE = 1024
T = 512
NPASS = TOK_CORE // T
D = 4096
NC = D // 128
H = 32
KV = 8
HD = 128
ISQ = float(1.0 / np.sqrt(128.0))
EPS = 1e-5
K_EXP = 64
RANK = 16
KR = K_EXP * RANK
NKRC = KR // 128
TOPK = 16
NT = T // 128
NBANK = T // 64
NGRP = T // 16


def build_nc():
    nc = bass.Bass("TRN2", target_bir_lowering=False)

    x0T = nc.dram_tensor("x0T", [D, TOK_CORE], F32, kind="ExternalInput").ap()
    wq = nc.dram_tensor("wq", [NC, 128, NC * 128], F32, kind="ExternalInput").ap()
    wk = nc.dram_tensor("wk", [KV, 128, NC * 128], F32, kind="ExternalInput").ap()
    wv = nc.dram_tensor("wv", [KV, 128, NC * 128], F32, kind="ExternalInput").ap()
    wo = nc.dram_tensor("wo", [NC, 128, NC * 128], F32, kind="ExternalInput").ap()
    aT = nc.dram_tensor("aT", [NKRC, 128, NC * 128], F32, kind="ExternalInput").ap()
    bkr = nc.dram_tensor("bkr", [NC, 128, NKRC * 128], F32, kind="ExternalInput").ap()
    wrT = nc.dram_tensor("wrT", [D, K_EXP], F32, kind="ExternalInput").ap()
    idf = nc.dram_tensor("idf", [128, 128], F32, kind="ExternalInput").ap()
    rsel = nc.dram_tensor("rsel", [K_EXP, KR], F32, kind="ExternalInput").ap()
    outT = nc.dram_tensor("outT", [D, TOK_CORE], F32, kind="ExternalOutput").ap()
    wc = nc.dram_tensor("wcache", [120, 128, 4096], BF16, kind="Internal").ap()

    with ExitStack() as stk:
        al = stk.enter_context
        r_fm = al(nc.sbuf_tensor("r_fm", [128, NC * T], BF16))
        x1 = al(nc.sbuf_tensor("x1", [128, NC * T], BF16))
        big = al(nc.sbuf_tensor("big", [128, NC * T], BF16))
        of = al(nc.sbuf_tensor("of", [128, NC * T], BF16))
        kf = al(nc.sbuf_tensor("kf", [128, KV * T], BF16))
        vf = al(nc.sbuf_tensor("vf", [128, KV * T], BF16))
        hf = al(nc.sbuf_tensor("hf", [128, NKRC * T], BF16))
        hgf = al(nc.sbuf_tensor("hgf", [128, NKRC * T], BF16))
        abd = al(nc.sbuf_tensor("abd", [128, 32 * (T // 4)], BF16))
        wbuf = al(nc.sbuf_tensor("wbuf", [128, 2 * 4096], BF16))
        wrs = al(nc.sbuf_tensor("wrs", [128, NC * K_EXP], BF16))
        rsels = al(nc.sbuf_tensor("rsels", [K_EXP, KR], BF16))
        idb = al(nc.sbuf_tensor("idb", [128, 128], BF16))
        idfs = al(nc.sbuf_tensor("idfs", [64, 64], F32))
        ones = al(nc.sbuf_tensor("ones", [128, 128], BF16))
        cst = al(nc.sbuf_tensor("cst", [128, 4], F32))
        mbf = al(nc.sbuf_tensor("mbf", [1, 2 * T], BF16))
        mrsave = al(nc.sbuf_tensor("mrsave", [1, T], F32))
        mr_bc = al(nc.sbuf_tensor("mr_bc", [128, T], BF16))
        rs_bc = al(nc.sbuf_tensor("rs_bc", [128, T], BF16))
        mr_bc4 = al(nc.sbuf_tensor("mr_bc4", [128, T], BF16))
        rs_bc4 = al(nc.sbuf_tensor("rs_bc4", [128, T], BF16))
        zsb = al(nc.sbuf_tensor("zsb", [128, 32], F32))
        zrr = al(nc.sbuf_tensor("zrr", [128, 32], F32))
        mxs = al(nc.sbuf_tensor("mxs", [128, 32], F32))
        stm = al(nc.sbuf_tensor("stm", [128, NT * 64], F32))
        swk = al(nc.sbuf_tensor("swk", [128, NT * 64], F32))
        msk = al(nc.sbuf_tensor("msk", [128, NT * 64], F32))
        mski = al(nc.sbuf_tensor("mski", [128, NT * 64], mybir.dt.int32))
        sel = al(nc.sbuf_tensor("sel", [128, NT * 64], F32))
        mxt = al(nc.sbuf_tensor("mxt", [128, NT], F32))
        nmx1 = al(nc.sbuf_tensor("nmx1", [128, NT], F32))
        gzs = al(nc.sbuf_tensor("gzs", [128, NT], F32))
        gzr = al(nc.sbuf_tensor("gzr", [128, NT], F32))
        gtm = al(nc.sbuf_tensor("gtm", [128, NT * 64], BF16))
        gfm = al(nc.sbuf_tensor("gfm", [K_EXP, T], BF16))
        ost = al(nc.sbuf_tensor("ost", [128, T], F32))
        abdt = hf[:, 0:2048]
        vstb = hf[:, 2048:2560]
        rowv = kf[:].bitcast(F32)

        pSC0 = al(nc.psum_tensor("pSC0", [128, 1024], F32))
        pSC1 = al(nc.psum_tensor("pSC1", [128, 1024], F32))
        pbB2 = al(nc.psum_tensor("pbB2", [128, 1024], F32))
        pbB = pbB2[:, 0:512]
        pbA = pSC0[:, 0:512]
        pbC = pSC0[:, 512:1024]
        pbS = pSC1[:, 0:512]
        pbQ = pSC1[:, 512:1024]
        ptT = al(nc.psum_tensor("ptT", [128, 1024], BF16))
        ptV2 = al(nc.psum_tensor("ptV2", [128, 384], BF16))
        ptV = ptV2[:, 0:256]
        ptG = ptV2[:, 256:384]

        spe = al(nc.semaphore("spe"))
        sdv = al(nc.semaphore("sdv"))
        sac = al(nc.semaphore("sac"))
        sgp = al(nc.semaphore("sgp"))
        NGCH = 11
        NSCH = 8
        gsem = [al(nc.semaphore("gch%d" % i)) for i in range(NGCH)]
        ssem = [al(nc.semaphore("sch%d" % i)) for i in range(NSCH)]
        block = al(nc.Block())

        cnt = {"spe": 0, "sdv": 0, "sac": 0, "sgp": 0}
        gcnt = [0] * NGCH
        scnt = [0] * NSCH
        sems = {"spe": spe, "sdv": sdv, "sac": sac, "sgp": sgp}

        EMIT = {"t": block.tensor, "v": block.vector, "s": block.scalar,
                "g": block.gpsimd, "y": block.sync}

        def _wait(e, deps):
            for d in deps:
                if d[0] == "g":
                    _, ch, n = d
                    if n:
                        e.wait_ge(gsem[ch], 16 * n)
                elif d[0] == "s2":
                    _, ch, n = d
                    if n:
                        e.wait_ge(ssem[ch], 16 * n)
                else:
                    sk, val = d
                    if val:
                        e.wait_ge(sems[sk], val)

        def ph(eng, deps, body):
            def f(e):
                _wait(e, deps)
                inst = body(e)
                if eng == "t":
                    inst.then_inc(spe, 1)
                elif eng == "v":
                    inst.then_inc(sdv, 1)
                elif eng == "s":
                    inst.then_inc(sac, 1)
                elif eng == "g":
                    inst.then_inc(sgp, 1)
            EMIT[eng](f)
            key = {"t": "spe", "v": "sdv", "s": "sac", "g": "sgp"}[eng]
            cnt[key] += 1
            return cnt[key]

        def dma(ch, deps, out, in_, accum=False):
            prev = gcnt[ch]
            def f(g):
                if prev:
                    g.wait_ge(gsem[ch], 16 * prev)
                _wait(g, deps)
                if accum:
                    g.dma_start(out=out, in_=in_,
                                accum_op=ALU.add).then_inc(gsem[ch], 16)
                else:
                    g.dma_start(out=out, in_=in_).then_inc(gsem[ch], 16)
            EMIT["g"](f)
            gcnt[ch] += 1
            return ("g", ch, gcnt[ch])

        def store(ch, deps, out, in_):
            prev = scnt[ch]
            def f(s):
                if prev:
                    s.wait_ge(ssem[ch], 16 * prev)
                _wait(s, deps)
                s.dma_start(out=out, in_=in_).then_inc(ssem[ch], 16)
            EMIT["y"](f)
            scnt[ch] += 1
            return ("s2", ch, scnt[ch])

        dma(0, [], idb[:], idf)
        dma(1, [], idfs[:], idf[0:64, 0:64])
        dma(2, [], rsels[:], rsel)
        m_const = dma(3, [], wrs[:].rearrange("p (c k) -> p c k", c=NC),
                      wrT.rearrange("(c p) k -> p c k", p=128))
        m_ones = ph("v", [], lambda v: v.memset(ones[:], 1.0))
        block.scalar(lambda s: s.add_instruction(
            mybir.InstLoadActFuncSet(
                name=nc.get_next_instruction_name(),
                act_func_set_id=6, ins=[], outs=[])))
        ph("v", [], lambda v: v.memset(cst[:, 0:1], -0.5))
        nc.const_aps.register(F32, -0.5, cst[:, 0:1])
        ph("v", [], lambda v: v.memset(cst[:, 1:2], ISQ))
        nc.const_aps.register(F32, ISQ, cst[:, 1:2])
        m_abd0 = ph("v", [], lambda v: v.memset(abd[:], 0.0))
        m_ptT0 = ph("v", [], lambda v: v.memset(ptT[:].bitcast(F32), 0.0))

        st8 = {"m_last_store": 0}

        def ln_stats(src, deps_src, eps, m_pe_stats=None,
                     chunk_ready=None, skip_mr_bc=False, save_mr=None,
                     add_mr=False, bc_out=None):
            if m_pe_stats is None:
                m_sq = {}
                for g in range(NC // 8):
                    sqb = [kf, vf][g % 2]
                    dd = list(deps_src)
                    if chunk_ready is not None:
                        cr = chunk_ready[8 * g + 7]
                        dd.append(cr if isinstance(cr, tuple) else ("sdv", cr))
                    if g >= 2:
                        dd.append(("spe", m_sq[g - 2]))
                    def sqg(v, g=g, sqb=sqb):
                        for i in range(8):
                            c = 8 * g + i
                            inst = v.tensor_tensor(
                                sqb[:, T * i:T * i + T],
                                src[:, T * c:T * c + T],
                                src[:, T * c:T * c + T], ALU.mult)
                        return inst
                    m_dv = ph("v", dd, sqg)
                    def smg(t, g=g, sqb=sqb):
                        for i in range(8):
                            c = 8 * g + i
                            t.matmul(pbS[0:1, :], ones[:, 0:1],
                                     src[:, T * c:T * c + T],
                                     start=(c == 0), stop=(c == NC - 1))
                        for i in range(8):
                            c = 8 * g + i
                            inst = t.matmul(pbQ[0:1, :], ones[:, 0:1],
                                            sqb[:, T * i:T * i + T],
                                            start=(c == 0), stop=(c == NC - 1))
                        return inst
                    dd2 = [("sdv", m_dv)]
                    if g == 0:
                        dd2.append(("sdv", m_ones))
                    m_sq[g] = ph("t", dd2, smg)
                m_pe = cnt["spe"]
            else:
                m_pe = m_pe_stats
            R = T
            rm = rowv[0:1, 0:T]
            rt = rowv[0:1, T:2 * T]
            ra = rowv[0:1, 2 * T:3 * T]
            m1 = ph("v", [("spe", m_pe)], lambda v: v.tensor_scalar(
                rm, pbS[0:1, :], 1.0 / D, None, ALU.mult))
            def e2m2(v):
                v.tensor_scalar(rt, pbQ[0:1, :], 1.0 / D, float(eps),
                                ALU.mult, ALU.add)
                return v.tensor_tensor(ra, rm, rm, ALU.mult)
            m3 = ph("v", [("sdv", m1)], e2m2)
            m4 = ph("v", [("sdv", m3)], lambda v: v.tensor_tensor(
                rt, rt, ra, ALU.subtract))
            m5 = ph("s", [("sdv", m4)], lambda s: s.activation(rt, rt, AF.Ln))
            m6 = ph("s", [("sac", m5)], lambda s: s.activation(
                rt, rt, AF.Exp, scale=-0.5))
            m7 = ph("v", [("sac", m6)], lambda v: v.tensor_tensor(
                ra, rm, rt, ALU.mult))
            if save_mr is not None:
                m7 = ph("v", [("sdv", m7)], lambda v: v.tensor_copy(
                    mrsave[0:1, :], ra))
            if add_mr:
                m7 = ph("v", [("sdv", m7)], lambda v: v.tensor_tensor(
                    ra, ra, mrsave[0:1, :], ALU.add))
            m8b = ph("v", [("sdv", m7)],
                     lambda v: v.tensor_copy(mbf[:, R:2 * R], rt))
            mb2 = ph("t", [("sdv", m8b)], lambda t: t.matmul(
                pbB[:, :], ones[0:1, :], mbf[0:1, R:2 * R], start=True,
                stop=True))
            mr_dst, rs_dst = bc_out if bc_out is not None else (mr_bc, rs_bc)
            me2m = ph("v", [("spe", mb2)],
                      lambda v: v.tensor_copy(rs_dst[:], pbB[:]))
            if skip_mr_bc:
                return me2m
            m8a = ph("v", [("sdv", m7)],
                     lambda v: v.tensor_copy(mbf[:, 0:R], ra))
            mb1 = ph("t", [("sdv", m8a)], lambda t: t.matmul(
                pbC[:, :], ones[0:1, :], mbf[0:1, 0:R], start=True, stop=True))
            me1 = ph("v", [("spe", mb1), ("sdv", me2m)],
                     lambda v: v.tensor_copy(mr_dst[:], pbC[:]))
            return me1

        def ln_normalize(src, dst, m_bc, chunk_guard=None):
            def p1(v):
                for c in range(NC):
                    if chunk_guard is not None:
                        tok = chunk_guard[c // (NC // 4)]
                        v.wait_ge(gsem[tok[1]], 16 * tok[2])
                    inst = v.tensor_tensor(dst[:, T * c:T * c + T],
                                           src[:, T * c:T * c + T], rs_bc[:],
                                           ALU.mult)
                return inst
            m1 = ph("v", [("sdv", m_bc)], p1)
            base = cnt["sdv"]
            def p2(v):
                v.wait_ge(sdv, m1)
                for c in range(NC):
                    v.tensor_tensor(dst[:, T * c:T * c + T],
                                    dst[:, T * c:T * c + T], mr_bc[:],
                                    ALU.subtract).then_inc(sdv, 1)
            EMIT["v"](p2)
            cnt["sdv"] += NC
            return cnt["sdv"], base

        def ln_residual(rbuf, y, m_bc):
            abs_c = {"n": cnt["sdv"]}
            spe_base = cnt["spe"]
            p2_mark = {}
            p3_mark = {}
            sqg_mark = {}
            def pall(v):
                v.wait_ge(sdv, m_bc)
                def p3(c):
                    v.wait_ge(sdv, p2_mark[c])
                    v.tensor_tensor(rbuf[:, T * c:T * c + T],
                                    rbuf[:, T * c:T * c + T],
                                    y[:, T * c:T * c + T],
                                    ALU.add).then_inc(sdv, 1)
                    abs_c["n"] += 1
                    p3_mark[c] = abs_c["n"]
                    if c % 8 == 7:
                        g = c // 8
                        sqb = [kf, vf][g % 2]
                        if g >= 2:
                            v.wait_ge(spe, spe_base + g - 1)
                        for i in range(8):
                            cc = 8 * g + i
                            v.wait_ge(sdv, p3_mark[cc])
                            v.tensor_tensor(sqb[:, T * i:T * i + T],
                                            rbuf[:, T * cc:T * cc + T],
                                            rbuf[:, T * cc:T * cc + T],
                                            ALU.mult).then_inc(sdv, 1)
                            abs_c["n"] += 1
                        sqg_mark[g] = abs_c["n"]
                last = 0
                for c in range(NC):
                    v.tensor_tensor(y[:, T * c:T * c + T],
                                    y[:, T * c:T * c + T], rs_bc[:],
                                    ALU.mult).then_inc(sdv, 1)
                    abs_c["n"] += 1
                    p2_mark[c] = abs_c["n"]
                    if c >= 1:
                        p3(c - 1)
                        last = c
                for c in range(last, NC):
                    p3(c)
            EMIT["v"](pall)
            cnt["sdv"] = abs_c["n"]
            return cnt["sdv"], sqg_mark, spe_base

        def ln3_stats_mms(sqg_mark, rbuf, spe_base):
            assert cnt["spe"] == spe_base, (cnt["spe"], spe_base)
            m_g = {}
            for g in range(NC // 8):
                sqb = [kf, vf][g % 2]
                def smg(t, g=g, sqb=sqb):
                    for i in range(8):
                        c = 8 * g + i
                        t.matmul(pbS[0:1, :], ones[:, 0:1],
                                 rbuf[:, T * c:T * c + T],
                                 start=(c == 0), stop=(c == NC - 1))
                    for i in range(8):
                        c = 8 * g + i
                        inst = t.matmul(pbQ[0:1, :], ones[:, 0:1],
                                        sqb[:, T * i:T * i + T],
                                        start=(c == 0), stop=(c == NC - 1))
                    return inst
                m_g[g] = ph("t", [("sdv", sqg_mark[g])], smg)
            return m_g[NC // 8 - 1]

        wb_guard = {"m": 0}
        wst_guard = {}

        def mm_stage(w_dram, njb, kchunks, rhs_fn, evac_fn, deps_rhs,
                     m_prev_bank=None, stats_dst=None, rhs_chunk=None,
                     cache=None, slots=None, slot_deps=None, evac_eng="v",
                     banks=None, jb_deps=None):
            m_dma_jb, m_mm_jb, m_ev_jb = {}, {}, {}
            m_sqmm = {}
            m_ws = {}
            slot, cmode = cache if cache is not None else (None, None)
            if banks is None:
                banks = [pbA, pbB, pbC]
            if slots is None:
                wslots = [wbuf[:, 0:kchunks * 128],
                          wbuf[:, 4096:4096 + kchunks * 128]]
            else:
                wslots = slots
            ns = len(wslots)
            for jb in range(njb):
                sidx = jb % ns
                wb = wslots[sidx]
                dd = []
                if jb >= ns:
                    dd.append(("spe", m_mm_jb[jb - ns]))
                    if cmode == "write":
                        dd.append(m_ws[jb - ns])
                else:
                    dd.append(("spe", wb_guard["m"]))
                    dd += list(wst_guard.values())
                    if slot_deps and sidx in slot_deps:
                        dd += list(slot_deps[sidx])
                if cmode == "read":
                    m_dma_jb[jb] = store(sidx, dd, wb,
                                         wc[slot + jb, :, 0:kchunks * 128])
                else:
                    m_dma_jb[jb] = dma(sidx, dd, wb, w_dram[jb, :, :])
                    if cmode == "write":
                        m_ws[jb] = store(sidx, [m_dma_jb[jb]],
                                         wc[slot + jb, :, 0:kchunks * 128], wb)
                        wst_guard[sidx] = m_ws[jb]
                bank = banks[jb % len(banks)]
                def mms(t, wb=wb, bank=bank, kchunks=kchunks, jb=jb):
                    for c in range(kchunks):
                        if rhs_chunk is not None and jb < 2:
                            t.wait_ge(sdv, rhs_chunk + c + 1)
                        inst = t.matmul(bank[:, :],
                                        wb[:, 128 * c:128 * c + 128],
                                        rhs_fn(c), start=(c == 0),
                                        stop=(c == kchunks - 1))
                    return inst
                dd = [m_dma_jb[jb]] + list(deps_rhs)
                if jb_deps and jb in jb_deps:
                    dd += list(jb_deps[jb])
                if jb >= len(banks):
                    dd.append(({"v": "sdv", "s": "sac"}[evac_eng],
                               m_ev_jb[jb - len(banks)]))
                elif m_prev_bank is not None:
                    dd.append(m_prev_bank)
                m_mm_jb[jb] = ph("t", dd, mms)
                m_ev_jb[jb] = ph(evac_eng, [("spe", m_mm_jb[jb])],
                                 lambda v, jb=jb, bank=bank: evac_fn(v, jb, bank))
                ek = {"v": "sdv", "s": "sac"}[evac_eng]
                if stats_dst is not None and jb % 8 == 7:
                    g = jb // 8
                    sqb = [kf, vf][g % 2]
                    dd = [(ek, m_ev_jb[jb])]
                    if g >= 2:
                        dd.append(("spe", m_sqmm[g - 2]))
                    def sqg(v, g=g, sqb=sqb):
                        for i in range(8):
                            c = 8 * g + i
                            inst = v.tensor_tensor(
                                sqb[:, T * i:T * i + T],
                                stats_dst[:, T * c:T * c + T],
                                stats_dst[:, T * c:T * c + T], ALU.mult)
                        return inst
                    m_sqdv = ph("v", dd, sqg)
                    def smg(t, g=g, sqb=sqb):
                        for i in range(8):
                            c = 8 * g + i
                            t.matmul(pbS[0:1, :], ones[:, 0:1],
                                     stats_dst[:, T * c:T * c + T],
                                     start=(c == 0), stop=(c == njb - 1))
                        for i in range(8):
                            c = 8 * g + i
                            inst = t.matmul(pbQ[0:1, :], ones[:, 0:1],
                                            sqb[:, T * i:T * i + T],
                                            start=(c == 0), stop=(c == njb - 1))
                        return inst
                    m_sqmm[g] = ph("t", [("sdv", m_sqdv)], smg)
            wb_guard["m"] = m_mm_jb[njb - 1]
            if stats_dst is not None:
                return m_ev_jb[njb - 1], m_sqmm[njb // 8 - 1]
            return m_ev_jb[njb - 1]

        m_prev_h = 0
        for ps in range(NPASS):
            t0 = T * ps
            R = [r_fm, x1][ps % 2]
            X = [x1, r_fm][ps % 2]

            if ps == 0:
                x0_toks = []
                for q in range(4):
                    c0 = 8 * q
                    x0_toks.append(dma(
                        8, [],
                        R[:].rearrange("p (c t) -> p c t", c=NC)[
                            :, c0:c0 + 8, :],
                        x0T.rearrange("(c p) t -> p c t", p=128)[
                            :, c0:c0 + 8, t0:t0 + T]))
                m_x0 = x0_toks[-1]
            else:
                m_x0 = st8["m_x0_next"]
                x0_toks = [m_x0] * 4
            deps_r = [m_x0]

            gdeps = [("spe", st8["m_st4_prev"])] if ps > 0 else []
            m_bc1 = ln_stats(R, gdeps, EPS,
                             chunk_ready=[x0_toks[c // 8] for c in range(NC)])
            if ps > 0:
                st8["emit_s12"]()
            m_x1, b_x1 = ln_normalize(
                R, X, m_bc1,
                chunk_guard=st8["b_addp_prev"] if ps else None)

            def evac_k(s, jb, bank):
                dst = kf[:].rearrange("p (t g) -> p g t", g=KV)[:, jb, :]
                return s.copy(dst, bank[:, :])
            m_k = mm_stage(wk, KV, NC, lambda c: X[:, T * c:T * c + T],
                           evac_k, [], rhs_chunk=b_x1, evac_eng="s",
                           slots=[wbuf[:, 0:4096], wbuf[:, 4096:8192], hf[:]],
                           slot_deps={2: [("sdv", st8.get("m_hg_prev", 0))]})
            def evac_v(s, jb, bank):
                dst = vf[:].rearrange("p (t g) -> p g t", g=KV)[:, jb, :]
                return s.copy(dst, bank[:, :])
            m_v = mm_stage(wv, KV, NC, lambda c: X[:, T * c:T * c + T],
                           evac_v, [("sdv", m_x1)], m_prev_bank=("sac", m_k),
                           evac_eng="s",
                           slots=[wbuf[:, 0:4096], wbuf[:, 4096:8192], hf[:]])
            def evac_q(s, jb, bank):
                if ps > 0:
                    tok = st8["m_out_add"]
                    s.wait_ge(gsem[tok[1]], 16 * tok[2])
                dst = big[:].rearrange("p (t h) -> p h t", h=H)[:, jb, :]
                return s.mul(dst, bank[:, :], ISQ)
            m_q = mm_stage(wq, H, NC, lambda c: X[:, T * c:T * c + T],
                           evac_q, [("sdv", m_x1)], m_prev_bank=("sac", m_v),
                           evac_eng="s",
                           slots=[wbuf[:, 0:4096], wbuf[:, 4096:8192], hf[:],
                                  hgf[:]],
                           slot_deps={2: [("sdv", st8.get("m_hg_prev", 0))]})

            NB2 = 4
            m_sm_done = {}
            m_abd_bank = {}
            m_ev_grp, m_av_grp, m_oev_grp = {}, {}, {}
            m_vev_grp = {}

            def emit_softmax(b2, bank, m_sc):
                def smax(v, bank=bank):
                    for u4 in range(4):
                        sv = bank[32 * u4:32 * u4 + 32, :].rearrange(
                            "p (w g) -> p w g", g=32)[:, :, 8 * u4:8 * u4 + 8]
                        inst = v.reduce_max(mxs[32 * u4:32 * u4 + 32, :], sv,
                                            axis=AX.X)
                    return inst
                m_mx = ph("v", [("spe", m_sc)], smax)
                def ssub(v, bank=bank):
                    for u4 in range(4):
                        sv = bank[32 * u4:32 * u4 + 32, :].rearrange(
                            "p (w g) -> p w g", g=32)[:, :, 8 * u4:8 * u4 + 8]
                        mb = mxs[32 * u4:32 * u4 + 32, :].unsqueeze(
                            2).broadcast_to([32, 32, 8])
                        inst = v.tensor_tensor(sv, sv, mb, ALU.subtract)
                    return inst
                m_sub = ph("v", [("sdv", m_mx)], ssub)
                def sexp(s, b2=b2, bank=bank):
                    for u4 in range(4):
                        sv = bank[32 * u4:32 * u4 + 32, :].rearrange(
                            "p (w g) -> p w g", g=32)[:, :, 8 * u4:8 * u4 + 8]
                        av = abd[32 * u4:32 * u4 + 32,
                                 1024 * b2:1024 * b2 + 1024].rearrange(
                            "p (w g) -> p w g", g=32)[:, :, 8 * u4:8 * u4 + 8]
                        inst = s.activation(av, sv, AF.Exp)
                    return inst
                dd = [("sdv", m_sub)]
                if ps == 0 and b2 == 0:
                    dd.append(("sdv", m_abd0))
                m_sm_done[b2] = ph("s", dd, sexp)
                def ssum(v, b2=b2):
                    for u4 in range(4):
                        av = abd[32 * u4:32 * u4 + 32,
                                 1024 * b2:1024 * b2 + 1024].rearrange(
                            "p (w g) -> p w g", g=32)[:, :, 8 * u4:8 * u4 + 8]
                        inst = v.reduce_sum(zsb[32 * u4:32 * u4 + 32, :], av,
                                            axis=AX.X)
                    return inst
                m_sum = ph("v", [("sac", m_sm_done[b2])], ssum)
                m_rec = ph("v", [("sdv", m_sum)],
                           lambda v: v.reciprocal(zrr[:, :], zsb[:, :]))
                def smul(v, b2=b2):
                    for u4 in range(4):
                        av = abd[32 * u4:32 * u4 + 32,
                                 1024 * b2:1024 * b2 + 1024].rearrange(
                            "p (w g) -> p w g", g=32)[:, :, 8 * u4:8 * u4 + 8]
                        zb = zrr[32 * u4:32 * u4 + 32, :].unsqueeze(
                            2).broadcast_to([32, 32, 8])
                        inst = v.tensor_tensor(av, av, zb, ALU.mult)
                    return inst
                m_abd_bank[b2] = ph("v", [("sdv", m_rec)], smul)

            def emit_grps(b2):
                m_abd_b = m_abd_bank[b2]
                for gp in range(4):
                    pair = 4 * b2 + gp
                    half = pair % 2
                    def trv(t, b2=b2, gp=gp, pair=pair):
                        if pair >= 1:
                            t.wait_ge(sdv, m_vev_grp[pair - 1])
                        for g2 in range(2):
                            grp = 8 * b2 + 2 * gp + g2
                            inst = t.transpose(ptV[:, 128 * g2:128 * g2 + 128],
                                               vf[:, 128 * grp:128 * grp + 128],
                                               idb[:])
                        return inst
                    m_trv = ph("t", [("sac", m_v)], trv)
                    def evv(v, pair=pair, half=half, m_trv=m_trv):
                        if pair >= 1:
                            v.wait_ge(spe, m_av_grp[pair - 1])
                        v.wait_ge(spe, m_trv)
                        return v.tensor_copy(
                            vstb[:, 256 * half:256 * half + 256], ptV[:])
                    m_vev_grp[pair] = ph("v", [], evv)
                    def trs(t, b2=b2, gp=gp, pair=pair, m_abd_b=m_abd_b):
                        if pair >= 1:
                            t.wait_ge(sdv, m_ev_grp[pair - 1])
                        elif ps == 0 and b2 == 0:
                            t.wait_ge(sdv, m_ptT0)
                        for g2 in range(2):
                            grp = 8 * b2 + 2 * gp + g2
                            for j in range(4):
                                w = 4 * grp + j
                                inst = t.transpose(
                                    ptT[32 * j:32 * j + 32,
                                        512 * g2 + 128 * j:512 * g2 + 128 * j + 128],
                                    abd[:, 32 * w:32 * w + 32], idb[:],
                                    tile_position=(0, 32 * j))
                        return inst
                    m_tr = ph("t", [("sdv", m_abd_b)], trs)
                    def evg(v, pair=pair, half=half, m_tr=m_tr):
                        if pair >= 1:
                            v.wait_ge(spe, m_av_grp[pair - 1])
                        v.wait_ge(spe, m_tr)
                        return v.tensor_copy(
                            abdt[:, 1024 * half:1024 * half + 1024], ptT[:])
                    m_ev_grp[pair] = ph("v", [], evg)
                    def avm(t, pair=pair, half=half):
                        if pair >= 1:
                            t.wait_ge(sac, m_oev_grp[pair - 1])
                        for g2 in range(2):
                            inst = t.matmul(
                                pbB2[:, 512 * g2:512 * g2 + 512],
                                vstb[:, 256 * half + 128 * g2:
                                     256 * half + 128 * g2 + 128],
                                abdt[:, 1024 * half + 512 * g2:
                                     1024 * half + 512 * g2 + 512],
                                start=True, stop=True)
                        return inst
                    m_av_grp[pair] = ph("t", [("sdv", m_ev_grp[pair]),
                                               ("sdv", m_vev_grp[pair])], avm)
                    def oev(s, pair=pair):
                        src_v = pbB2[:].rearrange(
                            "p (g t h) -> p h g t", g=2, t=16)
                        dst_v = of[:].rearrange("p (h q) -> p h q", h=H)[
                            :, :, 32 * pair:32 * pair + 32].rearrange(
                            "p h (g t) -> p h g t", g=2)
                        return s.copy(dst_v, src_v)
                    m_oev_grp[pair] = ph("s", [("spe", m_av_grp[pair])], oev)

            SB2 = [pSC0, pSC1]
            for b2 in range(NB2):
                bank = SB2[b2 % 2]
                def scmm(t, b2=b2, bank=bank):
                    for u in range(128):
                        tok = 128 * b2 + u
                        u4, w = u % 4, u // 4
                        inst = t.matmul(
                            bank[32 * u4:32 * u4 + 32,
                                 32 * w + 8 * u4:32 * w + 8 * u4 + 8],
                            big[:, H * tok:H * tok + H],
                            kf[:, KV * tok:KV * tok + KV],
                            start=True, stop=True, tile_position=(0, 32 * u4))
                    return inst
                dd = [("sac", m_q), ("sac", m_k)]
                if b2 >= 2:
                    dd.append(("sac", m_sm_done[b2 - 2]))
                m_sc = ph("t", dd, scmm)
                if b2 >= 1:
                    emit_grps(b2 - 1)
                emit_softmax(b2, bank, m_sc)
            emit_grps(NB2 - 1)
            m_attn_s = m_oev_grp[NGRP // 2 - 1]

            def evac_y(s, jb, bank):
                return s.copy(big[:, T * jb:T * jb + T], bank[:, :])
            m_y, m_st2 = mm_stage(wo, NC, NC, lambda c: of[:, T * c:T * c + T],
                                  evac_y, [("sac", m_attn_s)], stats_dst=big,
                                  evac_eng="s",
                                  slots=[wbuf[:, 0:4096], wbuf[:, 4096:8192],
                                         hf[:], hgf[:]],
                                  slot_deps={2: [("spe",
                                                  m_av_grp[NGRP // 2 - 1])]})
            m_bc2 = ln_stats(big, [("sac", m_y)], EPS, m_pe_stats=m_st2,
                             skip_mr_bc=True, save_mr=True)
            m_res2, sqg3, spe_b3 = ln_residual(R, big, m_bc2)

            m_st3 = ln3_stats_mms(sqg3, R, spe_b3)
            m_bc3 = ln_stats(R, [], EPS, m_pe_stats=m_st3)
            m_x3, b_x3 = ln_normalize(R, X, m_bc3)

            def rmm(t):
                for c in range(NC):
                    t.wait_ge(sdv, b_x3 + c + 1)
                    inst = t.matmul(pbA[0:K_EXP, :],
                                    wrs[:, K_EXP * c:K_EXP * c + K_EXP],
                                    X[:, T * c:T * c + T],
                                    start=(c == 0), stop=(c == NC - 1))
                return inst
            m_rt = ph("t", [m_const], rmm)
            m_rev = ph("v", [("spe", m_rt)],
                       lambda v: v.tensor_copy(ost[0:K_EXP, 0:T], pbA[0:K_EXP, :]))
            def rtp(t):
                for i in range(NT):
                    inst = t.transpose(pbC[:, 64 * i:64 * i + 64],
                                       ost[0:K_EXP, 128 * i:128 * i + 128],
                                       idfs[:, :])
                return inst
            m_rtp = ph("t", [("sdv", m_rev)], rtp)
            def rtm(v):
                v.tensor_copy(stm[:], pbC[:, 0:NT * 64])
                return v.tensor_copy(swk[:], pbC[:, 0:NT * 64])
            m_stm = ph("v", [("spe", m_rtp)], rtm)

            def evac_h(s, jb, bank):
                return s.copy(hf[:, T * jb:T * jb + T], bank[:, :])
            m_h = mm_stage(aT, NKRC, NC, lambda c: X[:, T * c:T * c + T],
                           evac_h, [], rhs_chunk=b_x3,
                           m_prev_bank=("sdv", m_rev),
                           cache=(80, "write" if ps == 0 else "read"),
                           evac_eng="s", banks=[pbA, pbB],
                           slots=[wbuf[:, 0:4096], wbuf[:, 4096:8192],
                                  of[:, 0:4096]])

            m_memneg = ph("v", [], lambda v: v.memset(sel[:], -1e30))
            sw3f = swk[:].rearrange("p (i k) -> p i k", k=64)
            st3f = stm[:].rearrange("p (i k) -> p i k", k=64)
            mi3f = mski[:].rearrange("p (i k) -> p i k", k=64)
            mxbf = mxt[:, :].unsqueeze(2).broadcast_to([128, NT, 64])
            m_nm1 = ph("v", [("sdv", m_stm)],
                       lambda v: v.reduce_max(gzs[:, :], st3f, axis=AX.X))
            m_nm2 = ph("v", [("sdv", m_nm1)], lambda v: v.tensor_scalar(
                nmx1[:], gzs[:], -1.0, None, ALU.mult))
            m_cur = m_stm
            round_marks = []
            for rnd in range(TOPK):
                m1 = ph("v", [("sdv", m_cur), ("sdv", m_memneg)],
                        lambda v: v.reduce_max(mxt[:, :], sw3f, axis=AX.X))
                m2 = ph("v", [("sdv", m1)],
                        lambda v: v.tensor_tensor(mi3f, sw3f, mxbf, ALU.is_ge))
                m_cur = ph("v", [("sdv", m2)],
                           lambda v: v.copy_predicated(swk[:], mski[:],
                                                       sel[:]))
                round_marks.append(m_cur)
            chain = {0: m_cur, 1: m_cur}
            m_topk = ph("v", [("sdv", chain[0]), ("sdv", chain[1])],
                        lambda v: v.tensor_scalar(msk[:], swk[:], -1e29, None,
                                                  ALU.is_lt))
            stm3 = stm[:].rearrange("p (i k) -> p i k", k=64)
            swk3 = swk[:].rearrange("p (i k) -> p i k", k=64)
            msk3 = msk[:].rearrange("p (i k) -> p i k", k=64)
            gtm3 = gtm[:].rearrange("p (i k) -> p i k", k=64)
            nmxb = nmx1[:, :].unsqueeze(2).broadcast_to([128, NT, 64])
            gzrb = gzr[:, :].unsqueeze(2).broadcast_to([128, NT, 64])
            m_gei = ph("v", [("sdv", m_topk)],
                       lambda v: v.tensor_tensor(swk3, stm3, nmxb, ALU.add))
            m_ge = ph("s", [("sdv", m_gei)],
                      lambda s: s.activation(swk[:], swk[:], AF.Exp))
            m_gm = ph("v", [("sac", m_ge)], lambda v: v.tensor_tensor(
                swk[:], swk[:], msk[:], ALU.mult))
            m_gs = ph("v", [("sdv", m_gm)],
                      lambda v: v.reduce_sum(gzs[:, :], swk3, axis=AX.X))
            m_gr = ph("v", [("sdv", m_gs)],
                      lambda v: v.reciprocal(gzr[:], gzs[:]))
            m_gt = ph("v", [("sdv", m_gr)],
                      lambda v: v.tensor_tensor(gtm3, swk3, gzrb, ALU.mult))
            m_gf = 0
            for i in range(NT):
                def gtp(t, i=i, m_gt=m_gt, m_gf=m_gf):
                    t.wait_ge(sdv, m_gt)
                    if i >= 1:
                        t.wait_ge(sdv, m_gf)
                    return t.transpose(ptG[0:K_EXP, :],
                                       gtm[:, 64 * i:64 * i + 64], idb[:])
                m_g1 = ph("t", [], gtp)
                m_gf = ph("v", [("spe", m_g1)], lambda v, i=i: v.tensor_copy(
                    gfm[:, 128 * i:128 * i + 128], ptG[0:K_EXP, :]))
            m_gfm = m_gf

            m_grc = {}
            GB = [pSC0[:, 0:512], pSC0[:, 512:1024],
                  pSC1[:, 0:512], pSC1[:, 512:1024]]
            for r2 in range(2):
                def grm(t, r2=r2):
                    if r2 == 1:
                        t.wait_ge(sdv, m_grc[0])
                    for i in range(4):
                        c = 4 * r2 + i
                        inst = t.matmul(GB[i][:, :],
                                        rsels[0:K_EXP, 128 * c:128 * c + 128],
                                        gfm[0:K_EXP, :], start=True, stop=True)
                    return inst
                m_g2 = ph("t", [("sdv", m_gfm), ("sac", m_h)], grm)
                def gmu(v, r2=r2, m_g2=m_g2):
                    v.wait_ge(spe, m_g2)
                    for i in range(4):
                        c = 4 * r2 + i
                        inst = v.tensor_tensor(hgf[:, T * c:T * c + T],
                                               hf[:, T * c:T * c + T],
                                               GB[i][:, :], ALU.mult)
                    return inst
                m_grc[r2] = ph("v", [], gmu)
            m_hg = m_grc[1]

            def evac_y2(s, jb, bank):
                return s.copy(big[:, T * jb:T * jb + T], bank[:, :])
            m_y2, m_st4 = mm_stage(bkr, NC, NKRC,
                                   lambda c: hgf[:, T * c:T * c + T],
                                   evac_y2, [("sdv", m_hg)], stats_dst=big,
                                   evac_eng="s",
                                   cache=(88, "write" if ps == 0 else "read"),
                                   slots=[wbuf[:, 1024 * i:1024 * i + 1024]
                                          for i in range(8)])
            if ps + 1 < NPASS:
                st8["m_x0_next"] = dma(
                    8, [("sac", m_h)],
                    X[:].rearrange("p (c t) -> p c t", c=NC),
                    x0T.rearrange("(c p) t -> p c t", p=128)[
                        :, :, t0 + T:t0 + 2 * T])

            m_bc4 = ln_stats(big, [("sac", m_y2)], EPS / 4.0,
                             m_pe_stats=m_st4, add_mr=True,
                             bc_out=(mr_bc4, rs_bc4))
            QW = NC // 4
            m_r_dma = {}
            outv = outT.rearrange("(c p) t -> p c t", p=128)
            for q in range(4):
                c0 = QW * q
                m_r_dma[q] = dma(9, [("sdv", m_res2)] if q == 0 else [],
                                 outv[:, c0:c0 + QW, t0:t0 + T],
                                 R[:].rearrange("p (c t) -> p c t", c=NC)[
                                     :, c0:c0 + QW, :])
            st8["b_addp_prev"] = m_r_dma

            def emit_s12_tail(Rc=R, bigc=big, m_bc4c=m_bc4, t0c=t0,
                              m_r_dmac=m_r_dma):
                def p1(v):
                    for c in range(NC):
                        inst = v.tensor_tensor(bigc[:, T * c:T * c + T],
                                               bigc[:, T * c:T * c + T],
                                               rs_bc4[:], ALU.mult)
                    return inst
                m_p1 = ph("v", [("sdv", m_bc4c)], p1)
                b_p2 = cnt["sdv"]
                def p2(v):
                    v.wait_ge(sdv, m_p1)
                    for c in range(NC):
                        v.tensor_tensor(bigc[:, T * c:T * c + T],
                                        bigc[:, T * c:T * c + T], mr_bc4[:],
                                        ALU.subtract).then_inc(sdv, 1)
                EMIT["v"](p2)
                cnt["sdv"] += NC
                m_add_last = 0
                for q in range(4):
                    c0 = QW * q
                    m_add_last = dma(
                        10, [("sdv", b_p2 + c0 + QW), m_r_dmac[q]],
                        outv[:, c0:c0 + QW, t0c:t0c + T],
                        bigc[:].rearrange("p (c t) -> p c t", c=NC)[
                            :, c0:c0 + QW, :],
                        accum=True)
                st8["m_out_add"] = m_add_last
            st8["emit_s12"] = emit_s12_tail
            st8["m_st4_prev"] = m_st4
            m_prev_h = m_h
            st8["m_hg_prev"] = m_hg
        st8["emit_s12"]()

        @block.sync
        def _(s):
            for i in range(NGCH):
                if gcnt[i]:
                    s.wait_ge(gsem[i], 16 * gcnt[i])
            for i in range(NSCH):
                if scnt[i]:
                    s.wait_ge(ssem[i], 16 * scnt[i])

    return nc


_NC_CACHE = None


def kernel(L0_attn_hidden_states, L0_attn_Wq, L0_attn_Wk, L0_attn_Wv,
           L0_attn_Wo, L0_ffn_A, L0_ffn_B, L0_ffn_Wr):
    global _NC_CACHE
    x = np.asarray(L0_attn_hidden_states, dtype=np.float32)
    wq = np.ascontiguousarray(np.asarray(L0_attn_Wq, dtype=np.float32))
    wk = np.ascontiguousarray(np.asarray(L0_attn_Wk, dtype=np.float32))
    wv = np.ascontiguousarray(np.asarray(L0_attn_Wv, dtype=np.float32))
    wo = np.ascontiguousarray(np.asarray(L0_attn_Wo, dtype=np.float32))
    A = np.asarray(L0_ffn_A, dtype=np.float32)
    B = np.asarray(L0_ffn_B, dtype=np.float32)
    Wr = np.asarray(L0_ffn_Wr, dtype=np.float32)

    def blk(W):
        Din, Dout = W.shape
        nc_, njb = Din // 128, Dout // 128
        return np.ascontiguousarray(
            W.reshape(nc_, 128, njb, 128).transpose(2, 1, 0, 3).reshape(
                njb, 128, nc_ * 128))

    x0T = np.ascontiguousarray(x.reshape(8192, 4096).T)
    wq = blk(wq)
    wk = blk(wk)
    wv = blk(wv)
    wo = blk(wo)
    aT = blk(np.ascontiguousarray(A.reshape(KR, D).T))
    bkr = blk(np.ascontiguousarray(B.transpose(0, 2, 1).reshape(KR, D)))
    wrT = np.ascontiguousarray(Wr.T)
    idf = np.eye(128, dtype=np.float32)
    rsel = np.zeros((K_EXP, KR), np.float32)
    for c in range(NKRC):
        for p in range(128):
            rsel[8 * c + p // 16, 128 * c + p] = 1.0

    if _NC_CACHE is None:
        _NC_CACHE = build_nc()
    nc = _NC_CACHE

    in_maps = []
    for core in range(NCORES):
        sl = np.ascontiguousarray(x0T[:, TOK_CORE * core:TOK_CORE * (core + 1)])
        in_maps.append({
            "x0T": sl, "wq": wq, "wk": wk, "wv": wv, "wo": wo,
            "aT": aT, "bkr": bkr, "wrT": wrT, "idf": idf, "rsel": rsel,
        })
    res = run_bass_kernel_spmd(nc, in_maps, core_ids=list(range(NCORES)))
    outT = np.concatenate([res.results[c]["outT"] for c in range(NCORES)],
                          axis=1)
    out = np.ascontiguousarray(outT.T).reshape(4, 2048, 4096).astype(np.float32)
    return out


if __name__ == "__main__":
    import reference as _ref
    inputs = _ref.setup_inputs()
    out = kernel(**{k: np.asarray(v) for k, v in inputs.items()})
    ref = np.asarray(_ref.reference(**inputs))
    num = np.sqrt(((out - ref) ** 2).mean())
    den = np.sqrt((ref ** 2).mean())
    print("rel l2:", num / den)
